# version 03
# baseline (speedup 1.0000x reference)
import numpy as np
import ml_dtypes
from contextlib import ExitStack

import concourse.bass as bass
import concourse.mybir as mybir
from concourse.bass_utils import run_bass_kernel_spmd

F32 = mybir.dt.float32
BF16 = mybir.dt.bfloat16
AF = mybir.ActivationFunctionType
ALU = mybir.AluOpType
AX = mybir.AxisListType
BF = ml_dtypes.bfloat16

MAGIC = float(np.float32(1.5 * 2 ** 23))
NCORES = 8
D = 2048
NH = 16
B = 2
S = 4096
TOK = 1024
NT = 8
EPS5 = 1e-5
EPS8 = 1e-8
ISQ = float(1.0 / np.sqrt(np.float32(128.0)))
H0 = 5632
H1 = 8192
BN = 512
BLK = 128 * BN
OFF0 = (0, 512, 1024, 1280)

ENGINES = ("pe", "act", "dve", "pool", "sp")

PREP = {
    "wq":   (D, D, "cast", "blk"),
    "wk":   (D, D, "cast", "blk"),
    "wv":   (D, D, "cast", "blk"),
    "wo":   (D, D, "cast", "blk"),
    "qkv":  (D, 3 * D, "cast", "blk"),
    "proj": (D, D, "cast", "blk"),
    "w1":   (D, H0, "q0", "blk"),
    "w3":   (D, H0, "q0", "blk"),
    "w2":   (H0, D, "q0", "rm"),
    "up":   (D, H1, "q1", "blk"),
    "gate": (D, H1, "q1", "blk"),
    "down": (H1, D, "q1", "rm"),
}
QMATS = ["w1", "w3", "w2", "up", "gate", "down"]


class Prog:

    SELF_SAFE = ("pe", "sp")

    def __init__(self, nc):
        self.nc = nc
        self.ops = {e: [] for e in ENGINES}
        self.cnt = {e: 0 for e in ENGINES}
        self.know = {e: {f: 0 for f in ENGINES} for e in ENGINES}
        self.snap = {}
        self.res = {}
        self.fence_req = None
        self.self_waited = {e: 0 for e in ENGINES}
        self.xsems = []
        self.xwaited = {e: set() for e in ENGINES}
        self.fence_x = 0

    def op(self, eng, fn, reads=(), writes=(), inc=1, xsem=None):
        deps = {}
        xdeps = set()
        selfdep = 0
        if self.fence_req is not None:
            for e, v in self.fence_req.items():
                if e != eng and v > 0:
                    deps[e] = max(deps.get(e, 0), v)
            for xi in range(self.fence_x):
                xdeps.add(xi)
        def adddep(e, v):
            nonlocal selfdep
            if e == "X":
                xdeps.add(v)
            elif e != eng:
                deps[e] = max(deps.get(e, 0), v)
            else:
                selfdep = max(selfdep, v)

        for r in reads:
            st = self.res.get(r)
            if st and st["w"]:
                adddep(*st["w"])
        for r in writes:
            st = self.res.get(r)
            if st:
                if st["w"]:
                    adddep(*st["w"])
                for (e, v) in st["r"]:
                    adddep(e, v)
        waits = []
        for xi in sorted(xdeps):
            if xi not in self.xwaited[eng]:
                waits.append(("X", xi))
                self.xwaited[eng].add(xi)
        if eng not in self.SELF_SAFE and selfdep > self.self_waited[eng]:
            waits.append((eng, selfdep))
            self.self_waited[eng] = selfdep
        k = self.know[eng]
        for e in ENGINES:
            v = deps.get(e, 0)
            if v <= k[e]:
                continue
            waits.append((e, v))
            sn = self.snap.get((e, v))
            if sn:
                for f, fv in sn.items():
                    if fv > k[f]:
                        k[f] = fv
            if v > k[e]:
                k[e] = v
        if xsem is not None:
            xid = len(self.xsems)
            self.xsems.append(None)
            self.ops[eng].append((waits, fn, ("X", xid)))
            rtag = wtag = ("X", xid)
            v_after = self.cnt[eng]
        else:
            self.cnt[eng] += inc
            v_after = self.cnt[eng]
            k[eng] = v_after
            self.ops[eng].append((waits, fn, inc))
            rtag = wtag = (eng, v_after)
            self.snap[(eng, v_after)] = dict(k)
        for r in reads:
            st = self.res.setdefault(r, {"w": None, "r": []})
            st["r"].append(rtag)
        for r in writes:
            st = self.res.setdefault(r, {"w": None, "r": []})
            st["w"] = wtag
            st["r"] = []
        return v_after

    def fence(self):
        self.fence_req = dict(self.cnt)
        self.fence_x = len(self.xsems)

    def fin(self, eng, dep_eng):
        v = self.cnt[dep_eng]
        if v > 0:
            self.ops[eng].append(([(dep_eng, v)], None, 0))

    def emit(self, ctx):
        nc = self.nc
        sems = {e: ctx.enter_context(nc.semaphore("sem_" + e)) for e in ENGINES}
        xs = [ctx.enter_context(nc.semaphore(f"xsem{i}")) for i in range(len(self.xsems))]

        block = ctx.enter_context(nc.Block())

        def make_body(ename):
            def body(eng):
                for waits, fn, inc in self.ops[ename]:
                    for (e, v) in waits:
                        if e == "X":
                            eng.wait_ge(xs[v], 1)
                        else:
                            eng.wait_ge(sems[e], v)
                    if fn is not None:
                        ins = fn(eng)
                        if isinstance(inc, tuple):
                            ins.then_inc(xs[inc[1]], 1)
                        elif inc:
                            ins.then_inc(sems[ename], inc)
            return body

        block.tensor(make_body("pe"))
        block.scalar(make_body("act"))
        block.vector(make_body("dve"))
        block.gpsimd(make_body("pool"))
        block.sync(make_body("sp"))


class Env:
    def __init__(self, nc):
        self.nc = nc
        self.prog = Prog(nc)
        self.h = {}

    def sb(self, ctx, name, shape, dtype):
        t = ctx.enter_context(self.nc.sbuf_tensor(name, shape, dtype))
        self.h[name] = t
        return t

    def ps(self, ctx, name, shape, dtype):
        t = ctx.enter_context(self.nc.psum_tensor(name, shape, dtype))
        self.h[name] = t
        return t

    def dram(self, name, shape, dtype, kind="Internal", addr_space="Local"):
        t = self.nc.dram_tensor(name, shape, dtype, kind=kind, addr_space=addr_space)
        self.h[name] = t
        return t


def AP_(h, offset, dims):
    return bass.AP(h, offset, dims)


def blk_src(handle, N, d0, nd, j0, nj):
    NJ = N // BN
    return AP_(handle, (d0 * NJ + j0) * BLK,
               [[BN, 128], [NJ * BLK, nd], [BLK, nj], [1, BN]])


def sb_dst(sb, col0, nd, nj, row_elems):
    return AP_(sb, col0, [[row_elems, 128], [nj * BN, nd], [BN, nj], [1, BN]])


def emit_ln_core(env, x_ap, x_res, t, out_cb, vc):
    p, h = env.prog, env.h
    SCR, VEC = h["SCR"], h["VEC"]
    vres = f"V{vc}"

    def V(i):
        return VEC[:, vc + i:vc + i + 1]

    p.op("act", lambda e: e.activation(SCR[:, :], x_ap, AF.Square, accum_out=V(0)),
         reads=[x_res], writes=["SCR", vres])
    p.op("dve", lambda e: e.reduce_sum(V(1), x_ap, AX.X),
         reads=[x_res], writes=[vres])
    p.op("dve", lambda e: e.tensor_scalar_mul(V(1), V(1), 1.0 / D),
         reads=[vres], writes=[vres])
    p.op("dve", lambda e: e.tensor_scalar(SCR[:, :], x_ap, V(1), None, ALU.subtract),
         reads=[x_res, vres], writes=["SCR"])
    p.op("dve", lambda e: e.tensor_mul(V(2), V(1), V(1)),
         reads=[vres], writes=[vres])
    p.op("dve", lambda e: e.tensor_scalar_add(V(2), V(2), -EPS5),
         reads=[vres], writes=[vres])
    p.op("dve", lambda e: e.tensor_scalar(V(0), V(0), 1.0 / D, V(2),
                                          ALU.mult, ALU.subtract),
         reads=[vres], writes=[vres])
    p.op("act", lambda e: e.activation(V(3), V(0), AF.Ln),
         reads=[vres], writes=[vres])
    p.op("act", lambda e: e.activation(V(4), V(3), AF.Exp, scale=-0.5),
         reads=[vres], writes=[vres])
    out_cb(V(4), vres)


def emit_ln_plain(env, x_ap, x_res, t, out_ap, out_res, vc, out_f32=False):
    p, h = env.prog, env.h

    def cb(rstd, vres):
        p.op("act", lambda e: e.activation(out_ap, h["SCR"][:, :], AF.Identity, scale=rstd),
             reads=["SCR", vres], writes=[out_res])

    emit_ln_core(env, x_ap, x_res, t, cb, vc)


def emit_ln_double(env, x_ap, x_res, t, out_ap, out_res, vc):
    p, h = env.prog, env.h
    SCR, SCR2 = h["SCR"], h["SCR2"]

    def cb1(rstd, vres):
        p.op("act", lambda e: e.activation(SCR2[:, :], SCR[:, :], AF.Identity, scale=rstd),
             reads=["SCR", vres], writes=["SCR2"])

    emit_ln_core(env, x_ap, x_res, t, cb1, vc)

    def cb2(rstd, vres):
        p.op("dve", lambda e: e.scalar_tensor_tensor(
            SCR[:, :], SCR[:, :], rstd, h["LNG"][:, :], ALU.mult, ALU.mult),
            reads=["SCR", vres, "LNG"], writes=["SCR"])
        p.op("dve", lambda e: e.tensor_add(out_ap, SCR[:, :], h["LNB"][:, :]),
             reads=["SCR", "LNB"], writes=[out_res])

    emit_ln_core(env, SCR2[:, :], "SCR2", t, cb2, vc + 8)


def emit_tile_transpose(env, src_sb, src_res, t):
    p, h = env.prog, env.h
    PT, IDENT, BIGA = h["PT"], h["IDENT"], h["BIGA"]
    for bi in range(2):
        for i in range(8):
            d = bi * 8 + i
            p.op("pe", lambda e, d=d, i=i: e.transpose(
                PT[:, i * 128:(i + 1) * 128],
                src_sb[:, d * 128:(d + 1) * 128], IDENT[:, :]),
                reads=[src_res, "IDENT"], writes=["PT"])
        dst = AP_(BIGA, (bi * 8) * TOK + t * 128, [[16384, 128], [TOK, 8], [1, 128]])
        p.op("dve", lambda e, dst=dst: e.tensor_copy(dst, PT[:, :]),
             reads=["PT"], writes=["BIGA"])


def emit_wsums(env):
    p, h = env.prog, env.h
    acc, win = h["VEC"], h["WIN"]
    p.op("dve", lambda e: e.memset(acc[:, 16:25], 0.0), writes=["wacc"])
    for mi, m in enumerate(QMATS):
        K, N, law, layout = PREP[m]
        rows = K // NCORES
        sl = h["qsl_" + m]
        for t in range((rows + 127) // 128):
            r0 = t * 128
            pr = min(128, rows - r0)
            src = AP_(sl, r0 * N, [[N, pr], [1, N]])
            p.op("sp", lambda e, src=src, pr=pr, N=N: e.dma_start(
                out=win[0:pr, 0:N], in_=src), writes=["WIN"], inc=16)
            p.op("dve", lambda e, pr=pr, N=N: e.tensor_reduce(
                acc[0:pr, 24:25], win[0:pr, 0:N], AX.X, ALU.add,
                apply_absolute_value=True),
                reads=["WIN"], writes=["wacc"])
            p.op("dve", lambda e, pr=pr, mi=mi: e.tensor_add(
                acc[0:pr, 16 + mi:17 + mi], acc[0:pr, 16 + mi:17 + mi],
                acc[0:pr, 24:25]),
                reads=["wacc"], writes=["wacc"])
    p.op("sp", lambda e: e.dma_start(out=h["wsum_scr"][:, :], in_=acc[:, 16:24]),
         reads=["wacc"], writes=["wsum_scr"], inc=16)
    p.op("sp", lambda e: e.dma_start(
        out=win[0:8, 0:128],
        in_=AP_(h["wsum_scr"], 0, [[1, 8], [8, 128]]),
        allow_slow_non_contiguous=True),
        reads=["wsum_scr"], writes=["WIN"], inc=16)
    p.op("dve", lambda e: e.reduce_sum(acc[0:8, 25:26], win[0:8, 0:128], AX.X),
         reads=["WIN"], writes=["wpart"])
    p.op("sp", lambda e: e.dma_start(out=h["part_out"][:, :], in_=acc[0:8, 25:26]),
         reads=["wpart"], writes=["part_out"], inc=16)


def emit_scales(env):
    p, h = env.prog, env.h
    scl = h["SCL"]
    p.op("sp", lambda e: e.dma_start(out=scl[:, 0:48], in_=h["partials"][:, :]),
         writes=["SCL"], inc=16)
    for i, m in enumerate(QMATS):
        K, N = PREP[m][0], PREP[m][1]
        p.op("dve", lambda e, i=i: e.reduce_sum(
            scl[:, 48 + i:49 + i], scl[:, 8 * i:8 * i + 8], AX.X),
            reads=["SCL"], writes=["SCL"])
        p.op("dve", lambda e, i=i, K=K, N=N: e.tensor_scalar_mul(
            scl[:, 54 + i:55 + i], scl[:, 48 + i:49 + i], 1.0 / (K * N)),
            reads=["SCL"], writes=["SCL"])
    p.op("dve", lambda e: e.tensor_scalar_max(scl[:, 0:3], scl[:, 54:57], EPS5),
         reads=["SCL"], writes=["SCL"])
    p.op("dve", lambda e: e.tensor_scalar_add(scl[:, 3:6], scl[:, 57:60], EPS8),
         reads=["SCL"], writes=["SCL"])
    p.op("dve", lambda e: e.reciprocal(scl[:, 8:14], scl[:, 0:6]),
         reads=["SCL"], writes=["SCL"])
    p.op("dve", lambda e: e.tensor_copy(h["RSB"][:, 0:16], scl[:, 0:16]),
         reads=["SCL"], writes=["RSB"])


def emit_prep(env):
    p, h = env.prog, env.h
    win, wout = h["WIN"], h["WOUT"]
    PA = [h[f"PA{i}"] for i in range(4)]
    for m, (K, N, law, layout) in PREP.items():
        rows = K // NCORES
        sl = h["qsl_" + m]
        out = h["qout_" + m]
        mi = QMATS.index(m) if m in QMATS else None
        for t in range((rows + 127) // 128):
            r0 = t * 128
            pr = min(128, rows - r0)
            if law == "cast":
                nj = N // BN
                src = AP_(sl, r0 * N, [[N, pr], [BN, nj], [1, BN]])
                dst = AP_(out, t * nj * BLK, [[BN, pr], [BLK, nj], [1, BN]])
                p.op("pool", lambda e, src=src, dst=dst: e.dma_start(out=dst, in_=src),
                     writes=["qout_" + m], inc=16)
                continue
            src = AP_(sl, r0 * N, [[N, pr], [1, N]])
            p.op("sp", lambda e, src=src, pr=pr, N=N: e.dma_start(
                out=win[0:pr, 0:N], in_=src), writes=["WIN"], inc=16)
            for q0 in range(0, N, 512):
                cw = min(512, N - q0)
                qi = (q0 // 512) % 4
                p.op("act", lambda e, pr=pr, q0=q0, cw=cw, qi=qi, mi=mi: e.activation(
                    PA[qi][0:pr, 0:cw], win[0:pr, q0:q0 + cw], AF.Identity,
                    bias=MAGIC, scale=h["RSB"][0:pr, 8 + mi:9 + mi]),
                    reads=["WIN", "RSB", "CONSTS"], writes=[f"PA{qi}"])
                p.op("dve", lambda e, pr=pr, q0=q0, cw=cw, qi=qi: e.tensor_scalar(
                    wout[0:pr, q0:q0 + cw], PA[qi][0:pr, 0:cw],
                    -MAGIC, 1.0, ALU.add, ALU.min),
                    reads=[f"PA{qi}"], writes=["WOUT"])
            p.op("dve", lambda e, pr=pr, N=N: e.tensor_scalar_max(
                wout[0:pr, 0:N], wout[0:pr, 0:N], -1.0),
                reads=["WOUT"], writes=["WOUT"])
            if layout == "blk":
                nj = N // BN
                dst = AP_(out, t * nj * BLK, [[BN, pr], [BLK, nj], [1, BN]])
                srcw = AP_(wout, 0, [[8192, pr], [BN, nj], [1, BN]])
                p.op("sp", lambda e, dst=dst, srcw=srcw: e.dma_start(out=dst, in_=srcw),
                     reads=["WOUT"], writes=["qout_" + m], inc=16)
            else:
                dst = AP_(out, r0 * N, [[N, pr], [1, N]])
                p.op("sp", lambda e, dst=dst, pr=pr, N=N: e.dma_start(
                    out=dst, in_=wout[0:pr, 0:N]),
                    reads=["WOUT"], writes=["qout_" + m], inc=16)



def emit_blk_load(env, eng, wname, NW, j0, nj, dst_col0, res):
    p, h = env.prog, env.h
    for jw in range(nj):
        dst = AP_(h[res], dst_col0 + jw * 512,
                  [[h[res].shape[1], 128], [nj * 512, 16], [1, 512]])
        src = blk_src(h[wname], NW, 0, 16, j0 + jw, 1)
        p.op(eng, lambda e, src=src, dst=dst: e.dma_start(out=dst, in_=src),
            writes=[res], inc=16)


def emit_lnqkv(env, layer):
    p, h = env.prog, env.h
    BIGA, BIGB = h["BIGA"], h["BIGB"]
    PA = [h[f"PA{i}"] for i in range(4)]
    WSTR, WHALF = h["WSTR"], h["WHALF"]

    for t in range(NT):
        xr_fetch(env, t)
        x_ap = xr_ap(env, t, 0, D)
        xnt = h["XNT0"] if t % 2 == 0 else h["XNT1"]
        xres = f"XNT{t % 2}"
        if layer == 0:
            emit_ln_double(env, x_ap, xr_res(env, t), t, xnt[:, :], xres, vc=(t % 2) * 16)
        else:
            emit_ln_plain(env, x_ap, xr_res(env, t), t, xnt[:, :], xres, vc=(t % 2) * 16)
        emit_tile_transpose(env, xnt, xres, t)
    p.fence()

    if layer == 0:
        plan = [("wq_b", D, 0, "Q", h["BQ"]), ("wk_b", D, 0, "K", h["BK"])]
    else:
        plan = [("qkv_b", 3 * D, 0, "Q", None), ("qkv_b", 3 * D, 4, "K", None)]
    for (wname, NW, base_j, which, bias_sb) in plan:
        for g in range(4):
            jb = base_j + g
            emit_blk_load(env, "sp", wname, NW, jb, 1, 0, "WSTR")
            for npair in range(2):
                for d in range(16):
                    for ni in range(2):
                        nl = npair * 2 + ni
                        for th in range(2):
                            pi = ni * 2 + th
                            p.op("pe", lambda e, d=d, nl=nl, th=th, pi=pi: e.matmul(
                                PA[pi][:, :],
                                WSTR[:, d * 512 + nl * 128:d * 512 + (nl + 1) * 128],
                                BIGA[:, d * TOK + th * 512:d * TOK + (th + 1) * 512],
                                start=(d == 0), stop=(d == 15)),
                                reads=["WSTR", "BIGA"], writes=[f"PA{pi}"])
                for ni in range(2):
                    nch = g * 4 + npair * 2 + ni
                    for th in range(2):
                        pi = ni * 2 + th
                        if which == "Q":
                            dst = BIGB[:, nch * TOK + th * 512:nch * TOK + (th + 1) * 512]
                            res = "BIGB"
                        else:
                            dst = h["STG0"][:, ni * 1024 + th * 512:ni * 1024 + th * 512 + 512]
                            res = "STG0"
                        if bias_sb is not None:
                            p.op("act", lambda e, dst=dst, pi=pi, nch=nch, bias_sb=bias_sb:
                                 e.activation(dst, PA[pi][:, :], AF.Identity,
                                              bias=bias_sb[:, nch:nch + 1]),
                                 reads=[f"PA{pi}", bias_sb.name], writes=[res])
                        else:
                            p.op("act", lambda e, dst=dst, pi=pi: e.activation(
                                dst, PA[pi][:, :], AF.Identity),
                                reads=[f"PA{pi}"], writes=[res])
                if which == "K":
                    for ni in range(2):
                        nch = g * 4 + npair * 2 + ni
                        kdst = AP_(h["ksh"], nch * 128 * TOK, [[TOK, 128], [1, TOK]])
                        ksrc = AP_(h["STG0"], ni * 1024, [[2048, 128], [1, TOK]])
                        p.op("sp", lambda e, kdst=kdst, ksrc=ksrc: e.dma_start(
                            out=kdst, in_=ksrc),
                            reads=["STG0"], writes=["ksh"], inc=16)
    p.fence()

    vwh = h["wv_b"] if layer == 0 else h["qkv_b"]
    vNW = D if layer == 0 else 3 * D
    vbase = 0 if layer == 0 else 8
    vwname = "wv_b" if layer == 0 else "qkv_b"
    for vh in range(2):
        emit_blk_load(env, "sp", vwname, vNW, vbase + vh * 2, 2, 0, "WHALF")
        for t in range(NT):
            last_pe = None
            for d in range(16):
                for nq in range(2):
                    p.op("pe", lambda e, d=d, t=t, nq=nq: e.matmul(
                        PA[nq][:, :],
                        BIGA[:, d * TOK + t * 128:d * TOK + (t + 1) * 128],
                        WHALF[:, d * 1024 + nq * 512:d * 1024 + (nq + 1) * 512],
                        start=(d == 0), stop=(d == 15 and layer != 0)),
                        reads=["WHALF", "BIGA"], writes=[f"PA{nq}"])
            for nq in range(2):
                if layer == 0:
                    p.op("pe", lambda e, nq=nq, vh=vh: e.matmul(
                        PA[nq][:, :], h["ONES1"][:, :],
                        AP_(h["BVROW"], vh * 1024 + nq * 512, [[2048, 1], [1, 512]]),
                        start=False, stop=True),
                        reads=["ONES1", "BVROW"], writes=[f"PA{nq}"])
                p.op("act", lambda e, nq=nq: e.activation(
                    h["STG1"][:, nq * 512:(nq + 1) * 512], PA[nq][:, :], AF.Identity),
                    reads=[f"PA{nq}"], writes=["STG1"])
            vdst = AP_(h["vsh"], t * 128 * D + vh * 1024, [[D, 128], [1, 1024]])
            p.op("sp", lambda e, vdst=vdst: e.dma_start(
                out=vdst, in_=h["STG1"][:, 0:1024]),
                reads=["STG1"], writes=["vsh"], inc=16)
    p.fence()


def emit_attn(env):
    p, h = env.prog, env.h
    BIGA, BIGB = h["BIGA"], h["BIGB"]
    KSB = h["KSB"]
    PA = [h[f"PA{i}"] for i in range(4)]
    PB, PT, IDENT = h["PB"], h["PT"], h["IDENT"]
    VEC, MROW, R2 = h["VEC"], h["MROW"], h["R2"]

    for vb in range(2):
        vs = h[f"VSB{vb}"]
        p.op("pool", lambda e, vs=vs: e.memset(
            AP_(vs, 128, [[4160, 128], [130, 32]]), 1.0), writes=[f"VSB{vb}"])
    p.op("pool", lambda e: e.memset(h["ONESC"][:, :], 1.0), writes=["ONESC"])

    def scores(r, b, hd, stop):
        for jj in range(4):
            nq = (4 - jj) * 128
            p.op("pe", lambda e, r=r, jj=jj, nq=nq, b=b, hd=hd, stop=stop: e.matmul(
                PB[:, OFF0[jj]:OFF0[jj] + nq],
                KSB[:, r * TOK + b * 512 + jj * 128:r * TOK + b * 512 + (jj + 1) * 128],
                BIGB[:, hd * TOK + b * 512 + jj * 128:hd * TOK + b * 512 + 512],
                start=(jj != 3), stop=stop),
                reads=["KSB", "BIGB"], writes=["PB"])

    def masks(at, atres, r):
        for jj in range(4):
            p.op("pool", lambda e, at=at, jj=jj, r=r: e.tensor_mul(
                at[:, OFF0[jj]:OFF0[jj] + 128],
                at[:, OFF0[jj]:OFF0[jj] + 128],
                h["MASK"][:, r * 128:(r + 1) * 128]),
                reads=[atres, "MASK"], writes=[atres])

    for hd in range(NH):
        src = AP_(h["kall"], hd * 128 * TOK,
                  [[TOK, 128], [2048 * TOK, 8], [1, TOK]])
        p.op("sp", lambda e, src=src: e.dma_start(
            out=AP_(KSB, 0, [[8192, 128], [TOK, 8], [1, TOK]]), in_=src),
            writes=["KSB"], inc=16)
        for b in range(2):
            vb = (hd * 2 + b) % 2
            vs = h[f"VSB{vb}"]
            vres = f"VSB{vb}"
            for r in range(8):
                src = AP_(h["vall"], (r * TOK + b * 512) * D + hd * 128,
                          [[D, 128], [128 * D, 4], [1, 128]])
                dst = AP_(vs, r * 4 * 130, [[4160, 128], [130, 4], [1, 128]])
                p.op("pool" if r % 2 else "sp",
                     lambda e, src=src, dst=dst: e.dma_start(out=dst, in_=src),
                     writes=[vres], inc=16)
            for r in range(8):
                at = h[f"AT{r % 2}"]
                atres = f"AT{r % 2}"
                scores(r, b, hd, stop=True)
                p.op("act", lambda e, at=at: e.activation(
                    at[:, 0:1408], PB[:, 0:1408], AF.Exp, scale=ISQ / 4.0),
                    reads=["PB"], writes=[atres])
                masks(at, atres, r)
                for jj in range(4):
                    nq = (4 - jj) * 128
                    p.op("pe", lambda e, at=at, jj=jj, nq=nq, r=r: e.matmul(
                        PA[0][0:1, jj * 128:jj * 128 + nq],
                        h["ONESC"][:, 0:1],
                        at[:, OFF0[jj]:OFF0[jj] + nq],
                        start=(r == 0 and jj == 0), stop=(r == 7 and jj == 3)),
                        reads=[atres, "ONESC"], writes=["PA0"])
            p.op("act", lambda e: e.activation(
                MROW[0:1, 0:512], PA[0][0:1, 0:512], AF.Ln),
                reads=["PA0"], writes=["MROW"])
            p.op("dve", lambda e: e.tensor_scalar_mul(
                MROW[0:1, 512:1024], MROW[0:1, 0:512], -4.0 / ISQ),
                reads=["MROW"], writes=["MROW"])
            for jj in range(4):
                nq = (4 - jj) * 128
                p.op("dve", lambda e, jj=jj, nq=nq: e.tensor_copy(
                    R2[0:1, OFF0[jj]:OFF0[jj] + nq],
                    MROW[0:1, 512 + jj * 128:512 + jj * 128 + nq]),
                    reads=["MROW"], writes=["R2"])
            for r in range(8):
                at = h[f"AT{r % 2}"]
                atres = f"AT{r % 2}"
                scores(r, b, hd, stop=False)
                for (s0, s1) in ((0, 512), (512, 1024), (1024, 1408)):
                    p.op("pe", lambda e, s0=s0, s1=s1: e.matmul(
                        PB[:, s0:s1], h["ONES1"][:, :], R2[0:1, s0:s1],
                        start=False, stop=(s0 == 1024)),
                        reads=["ONES1", "R2"], writes=["PB"])
                p.op("act", lambda e, at=at: e.activation(
                    at[:, 0:1408], PB[:, 0:1408], AF.Exp, scale=ISQ),
                    reads=["PB"], writes=[atres])
                masks(at, atres, r)
                for jj in range(4):
                    for kk in range(jj, 4):
                        p.op("pe", lambda e, at=at, jj=jj, kk=kk, r=r, vs=vs: e.matmul(
                            PA[kk][:, 0:129],
                            at[:, OFF0[jj] + (kk - jj) * 128:OFF0[jj] + (kk - jj + 1) * 128],
                            vs[:, (r * 4 + jj) * 130:(r * 4 + jj) * 130 + 129],
                            start=(r == 0 and jj == 0), stop=(r == 7 and jj == kk)),
                            reads=[atres, vres], writes=[f"PA{kk}"])
            for kk in range(4):
                av = h[f"AVT{kk % 2}"]
                avres = f"AVT{kk % 2}"
                p.op("dve", lambda e, kk=kk: e.reciprocal(
                    VEC[:, 40 + kk:41 + kk], PA[kk][:, 128:129]),
                    reads=[f"PA{kk}"], writes=["VRC"])
                p.op("act", lambda e, kk=kk, av=av: e.activation(
                    av[:, :], PA[kk][:, 0:128], AF.Identity, scale=VEC[:, 40 + kk:41 + kk]),
                    reads=[f"PA{kk}", "VRC"], writes=[avres])
                p.op("pe", lambda e, kk=kk, av=av: e.transpose(
                    PT[:, kk * 128:(kk + 1) * 128], av[:, :], IDENT[:, :]),
                    reads=[avres, "IDENT"], writes=["PT"])
            p.op("dve", lambda e, hd=hd, b=b: e.tensor_copy(
                BIGA[:, hd * TOK + b * 512:hd * TOK + b * 512 + 512], PT[:, 0:512]),
                reads=["PT"], writes=["BIGA"])
    p.fence()


def emit_oproj(env, layer):
    p, h = env.prog, env.h
    BIGA, WHALF = h["BIGA"], h["WHALF"]
    PA = [h[f"PA{i}"] for i in range(4)]
    woname = "wo_b" if layer == 0 else "proj_b"
    for nhalf in range(2):
        emit_blk_load(env, "sp", woname, D, nhalf * 2, 2, 0, "WHALF")
        for k in range(NT):
            for hd in range(16):
                for q in range(2):
                    p.op("pe", lambda e, hd=hd, k=k, q=q, layer=layer: e.matmul(
                        PA[q][:, :],
                        BIGA[:, hd * TOK + k * 128:hd * TOK + (k + 1) * 128],
                        WHALF[:, hd * 1024 + q * 512:hd * 1024 + (q + 1) * 512],
                        start=(hd == 0), stop=(hd == 15 and layer != 0)),
                        reads=["BIGA", "WHALF"], writes=[f"PA{q}"])
            xr_fetch(env, k, nhalf * 1024, (nhalf + 1) * 1024)
            for q in range(2):
                if layer == 0:
                    p.op("pe", lambda e, q=q, nhalf=nhalf: e.matmul(
                        PA[q][:, :], h["ONES1"][:, :],
                        AP_(h["BOROW"], nhalf * 1024 + q * 512, [[2048, 1], [1, 512]]),
                        start=False, stop=True),
                        reads=["ONES1", "BOROW"], writes=[f"PA{q}"])
                xa = xr_ap(env, k, nhalf * 1024 + q * 512, nhalf * 1024 + (q + 1) * 512)
                p.op("dve", lambda e, q=q, xa=xa: e.scalar_tensor_tensor(
                    xa, PA[q][:, :], 1.0, xa, ALU.mult, ALU.add),
                    reads=[f"PA{q}", xr_res(env, k)], writes=[xr_res(env, k)])
            xr_commit(env, k, nhalf * 1024, (nhalf + 1) * 1024)
    p.fence()


def emit_ffn0(env):
    p, h = env.prog, env.h
    BIGA, VEC = h["BIGA"], h["VEC"]
    PA = [h[f"PA{i}"] for i in range(4)]
    PT, IDENT = h["PT"], h["IDENT"]

    for t in range(NT):
        xr_fetch(env, t)
        x_ap = xr_ap(env, t, 0, D)
        vc = (t % 2) * 16
        emit_ln_plain(env, x_ap, xr_res(env, t), t, h["SCR2"][:, :], "SCR2", vc)
        vres = "QSCR"
        p.op("dve", lambda e: e.tensor_reduce(
            VEC[:, 48:49], h["SCR2"][:, :], AX.X, ALU.max, apply_absolute_value=True),
            reads=["SCR2"], writes=[vres])
        p.op("dve", lambda e: e.tensor_scalar_max(VEC[:, 48:49], VEC[:, 48:49], EPS5),
             reads=[vres], writes=[vres])
        p.op("dve", lambda e: e.reciprocal(VEC[:, 49:50], VEC[:, 48:49]),
             reads=[vres], writes=[vres])
        p.op("dve", lambda e: e.tensor_scalar_mul(VEC[:, 50:51], VEC[:, 49:50], 127.0),
             reads=[vres], writes=[vres])
        p.op("dve", lambda e: e.tensor_scalar_mul(VEC[:, 51:52], VEC[:, 48:49], 1.0 / 127.0),
             reads=[vres], writes=[vres])
        p.op("dve", lambda e, t=t: e.tensor_mul(
            VEC[:, 64 + 2 * t:65 + 2 * t], VEC[:, 51:52], h["RSB"][:, 0:1]),
            reads=[vres, "RSB"], writes=[f"D1_{t}"])
        p.op("dve", lambda e, t=t: e.tensor_mul(
            VEC[:, 65 + 2 * t:66 + 2 * t], VEC[:, 51:52], h["RSB"][:, 1:2]),
            reads=[vres, "RSB"], writes=[f"D1_{t}"])
        xnt = h["XNT0"] if t % 2 == 0 else h["XNT1"]
        xres = f"XNT{t % 2}"
        for q in range(4):
            p.op("act", lambda e, q=q: e.activation(
                PA[q][:, :], h["SCR2"][:, q * 512:(q + 1) * 512], AF.Identity,
                bias=MAGIC, scale=VEC[:, 50:51]),
                reads=["SCR2", vres, "CONSTS"], writes=[f"PA{q}"])
            p.op("dve", lambda e, q=q, xnt=xnt: e.tensor_scalar(
                xnt[:, q * 512:(q + 1) * 512], PA[q][:, :],
                -MAGIC, 127.0, ALU.add, ALU.min),
                reads=[f"PA{q}"], writes=[xres])
        p.op("dve", lambda e, xnt=xnt: e.tensor_scalar_max(xnt[:, :], xnt[:, :], -128.0),
             reads=[xres], writes=[xres])
        emit_tile_transpose(env, xnt, xres, t)
    p.fence()

    WFB, SILU, HB = h["WFB"], h["SILU"], h["HB"]
    for tg in range(4):
        t0 = tg * 2
        for (mat, mi, dst, dstres, kind) in (
                ("w1q_b", 0, SILU, "SILU", "silu"), ("w3q_b", 1, HB, "HB", "gate")):
            for (j0, gw) in env.hgroups:
                emit_blk_load(env, "sp", mat, H0, j0, gw, 0, "WFB")
                for d in range(16):
                    for u in range(2):
                        t = t0 + u
                        for jw in range(gw):
                            pi = u * 2 + jw
                            p.op("pe", lambda e, d=d, t=t, jw=jw, pi=pi, gw=gw: e.matmul(
                                PA[pi][:, :],
                                BIGA[:, d * TOK + t * 128:d * TOK + (t + 1) * 128],
                                h["WFB"][:, d * gw * 512 + jw * 512:d * gw * 512 + (jw + 1) * 512],
                                start=(d == 0), stop=(d == 15)),
                                reads=["WFB", "BIGA"], writes=[f"PA{pi}"])
                for u in range(2):
                    t = t0 + u
                    for jw in range(gw):
                        pi = u * 2 + jw
                        col = u * H0 + (j0 + jw) * 512
                        if kind == "silu":
                            p.op("act", lambda e, pi=pi, col=col, t=t: e.activation(
                                SILU[:, col:col + 512], PA[pi][:, :], AF.Silu,
                                scale=VEC[:, 64 + 2 * t:65 + 2 * t]),
                                reads=[f"PA{pi}", f"D1_{t}"], writes=["SILU"])
                        else:
                            p.op("dve", lambda e, pi=pi, col=col, t=t: e.scalar_tensor_tensor(
                                HB[:, col:col + 512], PA[pi][:, :],
                                VEC[:, 65 + 2 * t:66 + 2 * t],
                                SILU[:, col:col + 512], ALU.mult, ALU.mult),
                                reads=[f"PA{pi}", f"D1_{t}", "SILU"], writes=["HB"])
        for u in range(2):
            t = t0 + u
            hb_ap = HB[:, u * H0:(u + 1) * H0]
            vres = "QSCR"
            p.op("dve", lambda e, hb_ap=hb_ap: e.tensor_reduce(
                VEC[:, 48:49], hb_ap, AX.X, ALU.max, apply_absolute_value=True),
                reads=["HB"], writes=[vres])
            p.op("dve", lambda e: e.tensor_scalar_max(VEC[:, 48:49], VEC[:, 48:49], EPS5),
                 reads=[vres], writes=[vres])
            p.op("dve", lambda e: e.reciprocal(VEC[:, 49:50], VEC[:, 48:49]),
                 reads=[vres], writes=[vres])
            p.op("dve", lambda e: e.tensor_scalar_mul(VEC[:, 50:51], VEC[:, 49:50], 127.0),
                 reads=[vres], writes=[vres])
            p.op("dve", lambda e: e.tensor_scalar_mul(VEC[:, 51:52], VEC[:, 48:49], 1.0 / 127.0),
                 reads=[vres], writes=[vres])
            p.op("dve", lambda e, t=t: e.tensor_mul(
                VEC[:, 80:81], VEC[:, 51:52], h["RSB"][:, 2:3]),
                reads=[vres, "RSB"], writes=["D2"])
            HQ = h["HQ"]
            for q0 in range(0, H0, 512):
                qi = (q0 // 512) % 4
                p.op("act", lambda e, q0=q0, qi=qi, hb_ap=hb_ap: e.activation(
                    PA[qi][:, :], hb_ap[:, q0:q0 + 512], AF.Identity,
                    bias=MAGIC, scale=VEC[:, 50:51]),
                    reads=["HB", vres, "CONSTS"], writes=[f"PA{qi}"])
                p.op("dve", lambda e, q0=q0, qi=qi: e.tensor_scalar(
                    HQ[:, q0:q0 + 512], PA[qi][:, :],
                    -MAGIC, 127.0, ALU.add, ALU.min),
                    reads=[f"PA{qi}"], writes=["SILU"])
            p.op("dve", lambda e: e.tensor_scalar_max(HQ[:, :], HQ[:, :], -128.0),
                 reads=["SILU"], writes=["SILU"])
            for e4 in range(11):
                wb = h["WB20"]
                wbres = "WFB"
                w2src = AP_(h["w2q_rm"], e4 * 4 * 128 * D,
                            [[D, 128], [128 * D, 4], [1, D]])
                p.op("pool" if e4 % 2 else "sp",
                     lambda e, wb=wb, w2src=w2src: e.dma_start(
                         out=AP_(wb, 0, [[8192, 128], [2048, 4], [1, 2048]]),
                         in_=w2src),
                     writes=[wbres], inc=16)
                for es in range(4):
                    ec = e4 * 4 + es
                    sl = ec % 4
                    p.op("pe", lambda e, ec=ec, sl=sl: e.transpose(
                        PT[:, sl * 128:(sl + 1) * 128],
                        HQ[:, ec * 128:(ec + 1) * 128], IDENT[:, :]),
                        reads=["SILU", "IDENT"], writes=[f"PTs{sl}"])
                    p.op("dve", lambda e, sl=sl: e.tensor_copy(
                        h["HQT"][:, sl * 128:(sl + 1) * 128], PT[:, sl * 128:(sl + 1) * 128]),
                        reads=[f"PTs{sl}"], writes=[f"HQTs{sl}"])
                    for q in range(4):
                        p.op("pe", lambda e, ec=ec, sl=sl, q=q, es=es, wb=wb: e.matmul(
                            PA[q][:, :],
                            h["HQT"][:, sl * 128:(sl + 1) * 128],
                            wb[:, es * 2048 + q * 512:es * 2048 + (q + 1) * 512],
                            start=(ec == 0), stop=(ec == 43)),
                            reads=[f"HQTs{sl}", wbres], writes=[f"PA{q}"])
            xr_fetch(env, t)
            for q in range(4):
                xa = xr_ap(env, t, q * 512, (q + 1) * 512)
                p.op("dve", lambda e, q=q, xa=xa: e.scalar_tensor_tensor(
                    xa, PA[q][:, :], VEC[:, 80:81], xa, ALU.mult, ALU.add),
                    reads=[f"PA{q}", "D2", xr_res(env, t)], writes=[xr_res(env, t)])
            xr_commit(env, t)
    p.fence()


def emit_ffn1(env, ncap=4):
    p, h = env.prog, env.h
    BIGA, BIGB, VEC = h["BIGA"], h["BIGB"], h["VEC"]
    PA = [h[f"PA{i}"] for i in range(4)]

    for t in range(NT):
        xr_fetch(env, t)
        x_ap = xr_ap(env, t, 0, D)
        xnt = h["XNT0"] if t % 2 == 0 else h["XNT1"]
        xres = f"XNT{t % 2}"
        emit_ln_plain(env, x_ap, xr_res(env, t), t, xnt[:, :], xres, vc=(t % 2) * 16)
        emit_tile_transpose(env, xnt, xres, t)
    p.fence()

    for Q in range(4):
        for j in range(4):
            jg = Q * 4 + j
            emit_blk_load(env, "sp", "upq_b", H1, jg, 1, 0, "WUPJ")
            emit_blk_load(env, "pool", "gateq_b", H1, jg, 1, 0, "WGJ")
            for es in range(4):
                for d in range(16):
                    for th in range(2):
                        p.op("pe", lambda e, d=d, es=es, th=th: e.matmul(
                            PA[th][:, :],
                            h["WUPJ"][:, d * 512 + es * 128:d * 512 + (es + 1) * 128],
                            BIGA[:, d * TOK + th * 512:d * TOK + (th + 1) * 512],
                            start=(d == 0), stop=(d == 15)),
                            reads=["WUPJ", "BIGA"], writes=[f"PA{th}"])
                        p.op("pe", lambda e, d=d, es=es, th=th: e.matmul(
                            PA[2 + th][:, :],
                            h["WGJ"][:, d * 512 + es * 128:d * 512 + (es + 1) * 128],
                            BIGA[:, d * TOK + th * 512:d * TOK + (th + 1) * 512],
                            start=(d == 0), stop=(d == 15)),
                            reads=["WGJ", "BIGA"], writes=[f"PA{2 + th}"])
                ech = j * 4 + es
                for th in range(2):
                    p.op("act", lambda e, th=th: e.activation(
                        h["SILU1"][:, th * 512:(th + 1) * 512], PA[th][:, :], AF.Silu,
                        scale=h["RSB"][:, 3:4]),
                        reads=[f"PA{th}", "RSB"], writes=["SILU1"])
                    p.op("dve", lambda e, th=th, ech=ech: e.scalar_tensor_tensor(
                        BIGB[:, ech * TOK + th * 512:ech * TOK + (th + 1) * 512],
                        PA[2 + th][:, :], h["RSB"][:, 4:5],
                        h["SILU1"][:, th * 512:(th + 1) * 512], ALU.mult, ALU.mult),
                        reads=[f"PA{2 + th}", "RSB", "SILU1"], writes=["BIGB"])
        for half in range(16 // ncap):
            wdsrc = AP_(h["downq_rm"], (Q * 2048 + half * 128 * ncap) * D,
                        [[D, 128], [128 * D, ncap], [1, D]])
            p.op("sp", lambda e, wdsrc=wdsrc: e.dma_start(
                out=AP_(h["WDNH"], 0, [[2048 * ncap, 128], [2048, ncap], [1, 2048]]),
                in_=wdsrc),
                writes=["WDNH"], inc=16)
            for t in range(NT):
                for e8 in range(ncap):
                    ech = half * ncap + e8
                    for q in range(4):
                        p.op("pe", lambda e, ech=ech, e8=e8, t=t, q=q: e.matmul(
                            PA[q][:, :],
                            BIGB[:, ech * TOK + t * 128:ech * TOK + (t + 1) * 128],
                            h["WDNH"][:, e8 * 2048 + q * 512:e8 * 2048 + (q + 1) * 512],
                            start=(e8 == 0), stop=(e8 == ncap - 1)),
                            reads=["BIGB", "WDNH"], writes=[f"PA{q}"])
                xr_fetch(env, t)
                for q in range(4):
                    xa = xr_ap(env, t, q * 512, (q + 1) * 512)
                    p.op("dve", lambda e, q=q, xa=xa, t=t: e.scalar_tensor_tensor(
                        xa, PA[q][:, :], h["RSB"][:, 5:6], xa, ALU.mult, ALU.add),
                        reads=[f"PA{q}", "RSB", xr_res(env, t)], writes=[xr_res(env, t)])
                xr_commit(env, t)
    p.fence()


def setup_xr(env, mode):
    env.xr_mode = mode
    env.xr_state = {}


def xr_ap(env, t, c0, c1):
    if env.xr_mode == "resident":
        return env.h["XR"][:, t * D + c0:t * D + c1]
    return env.h["XTILE"][:, (t % 2) * D + c0:(t % 2) * D + c1]


def xr_res(env, t):
    return "XR" if env.xr_mode == "resident" else f"XT{t % 2}"


def xr_fetch(env, t, c0=0, c1=D):
    if env.xr_mode == "resident":
        return
    p = env.prog
    key = (t, c0)
    srcname = "xout_d" if env.xr_state.get(key, 0) > 0 else "xin_d"
    src = AP_(env.h[srcname], t * 128 * D + c0, [[D, 128], [1, c1 - c0]])
    dst = xr_ap(env, t, c0, c1)
    p.op("sp", lambda e, src=src, dst=dst: e.dma_start(out=dst, in_=src),
         reads=[f"xd_{t}_{c0}"], writes=[xr_res(env, t)], inc=16)


def xr_commit(env, t, c0=0, c1=D):
    if env.xr_mode == "resident":
        return
    p = env.prog
    key = (t, c0)
    env.xr_state[key] = env.xr_state.get(key, 0) + 1
    dst = AP_(env.h["xout_d"], t * 128 * D + c0, [[D, 128], [1, c1 - c0]])
    src = xr_ap(env, t, c0, c1)
    p.op("sp", lambda e, src=src, dst=dst: e.dma_start(out=dst, in_=src),
         reads=[xr_res(env, t)], writes=[f"xd_{t}_{c0}"], inc=16)


BUFS = {
    "XR":    ([128, 16384], F32),
    "XTILE": ([128, 4096], F32),
    "BIGA":  ([128, 16384], BF16),
    "BIGB":  ([128, 16384], BF16),
    "SCR":   ([128, 2048], F32),
    "SCR2":  ([128, 2048], F32),
    "XNT0":  ([128, 2048], BF16),
    "XNT1":  ([128, 2048], BF16),
    "VEC":   ([128, 256], F32),
    "SCL":   ([128, 64], F32),
    "RSB":   ([128, 16], F32),
    "IDENT": ([128, 128], BF16),
    "IDENT32": ([128, 128], F32),
    "ONES1": ([1, 128], BF16),
    "LNG":   ([128, 2048], F32),
    "LNB":   ([128, 2048], F32),
    "BQ":    ([128, 16], F32),
    "BK":    ([128, 16], F32),
    "BVROW": ([1, 2048], BF16),
    "BOROW": ([1, 2048], BF16),
    "MASK":  ([128, 1024], BF16),
    "MROW":  ([1, 1024], F32),
    "R2":    ([1, 1536], BF16),
    "ONESC": ([128, 1], BF16),
    "WSTR":  ([128, 8192], BF16),
    "WHALF": ([128, 16384], BF16),
    "STG0":  ([128, 2048], BF16),
    "STG1":  ([128, 2048], BF16),
    "KSB":   ([128, 8192], BF16),
    "VSB0":  ([128, 4160], BF16),
    "VSB1":  ([128, 4160], BF16),
    "AT0":   ([128, 1408], BF16),
    "AT1":   ([128, 1408], BF16),
    "AVT0":  ([128, 128], BF16),
    "AVT1":  ([128, 128], BF16),
    "WIN":   ([128, 8192], F32),
    "WOUT":  ([128, 8192], BF16),
    "WFB":   ([128, 8192], BF16),
    "SILU":  ([128, 11264], BF16),
    "HB":    ([128, 11264], BF16),
    "HQ":    ([128, 5632], BF16),
    "HQT":   ([128, 512], BF16),
    "WB20":  ([128, 8192], BF16),
    "WUPJ":  ([128, 8192], BF16),
    "WGJ":   ([128, 8192], BF16),
    "WDNH":  ([128, 8192], BF16),
    "SILU1": ([128, 1024], F32),
}

CONST_INS = {
    "IDENT": lambda cfg, c: np.eye(128, dtype=BF),
    "IDENT32": lambda cfg, c: np.eye(128, dtype=np.float32),
    "ONES1": lambda cfg, c: np.ones((1, 128), dtype=BF),
    "MASK": lambda cfg, c: host_masks(c),
}


def host_masks(c):
    m = np.zeros((128, NCORES * 128), dtype=BF)
    kp = np.arange(128)[:, None]
    qp = np.arange(128)[None, :]
    for r in range(NCORES):
        if r < c:
            m[:, r * 128:(r + 1) * 128] = 1
        elif r == c:
            m[:, r * 128:(r + 1) * 128] = (kp <= qp).astype(BF)
    return m


def build_program(bufs, drams, body, xr_mode="stream", load_consts=()):
    nc = bass.Bass("TRN2", target_bir_lowering=False)
    env = Env(nc)
    setup_xr(env, xr_mode)
    for (name, shape, dt, kind) in drams:
        env.dram(name, shape, dt, kind=kind)
    with ExitStack() as ctx:
        for nmb in bufs:
            env.sb(ctx, nmb, *BUFS[nmb])
        for i in range(4):
            env.ps(ctx, f"PA{i}", [128, 512], F32)
        env.ps(ctx, "PB", [128, 1536], F32)
        env.ps(ctx, "PT", [128, 1024], BF16)
        p = env.prog
        CON = env.sb(ctx, "CONSTS", [128, 4], F32)
        p.op("pool", lambda e: e.memset(CON[:, 0:1], MAGIC), writes=["CONSTS"])
        nc.const_aps.register(F32, MAGIC, CON[:, 0:1])
        for cname in load_consts:
            hsb = env.h[cname]
            hdr = env.h[cname + "_d"]
            p.op("sp", lambda e, hsb=hsb, hdr=hdr: e.dma_start(
                out=hsb[:, :], in_=hdr[:, :]), writes=[cname], inc=16)
        body(env)
        p.fin("sp", "sp")
        p.fin("sp", "pool")
        p.emit(ctx)
    return env, nc


def dram_in(env_drams, name, shape, dtype):
    env_drams.append((name, shape, dtype, "ExternalInput"))


def dram_out(env_drams, name, shape, dtype):
    env_drams.append((name, shape, dtype, "ExternalOutput"))


def body_w1(env):
    emit_wsums(env)


def body_w2(env):
    emit_scales(env)
    emit_prep(env)


def body_lnqkv(env, layer):
    p = env.prog
    emit_lnqkv(env, layer)
    p.op("sp", lambda e: e.dma_start(out=env.h["qsb_out"][:, :], in_=env.h["BIGB"][:, :]),
         reads=["BIGB"], writes=["qsb_out"], inc=16)


def body_attn(env):
    p = env.prog
    p.op("sp", lambda e: e.dma_start(out=env.h["BIGB"][:, :], in_=env.h["qsb_in"][:, :]),
         writes=["BIGB"], inc=16)
    emit_attn(env)
    p.op("sp", lambda e: e.dma_start(out=env.h["avf_out"][:, :], in_=env.h["BIGA"][:, :]),
         reads=["BIGA"], writes=["avf_out"], inc=16)


def body_oproj(env, layer):
    p = env.prog
    p.op("sp", lambda e: e.dma_start(out=env.h["BIGA"][:, :], in_=env.h["avf_in"][:, :]),
         writes=["BIGA"], inc=16)
    emit_oproj(env, layer)


def body_ffn0(env):
    emit_scales(env)
    emit_ffn0(env)


def body_ffn1(env):
    emit_scales(env)
    emit_ffn1(env)


def shard_row_indices(c):
    ks = np.arange(NT)
    return ((c + NCORES * ks)[:, None] * 128 + np.arange(128)[None, :]).reshape(-1)


def _f32(a):
    return np.ascontiguousarray(np.asarray(a, dtype=np.float32))


_PROG_CACHE = {}
DEBUG_CAPTURE = None


def _cap(k, v):
    if DEBUG_CAPTURE is not None:
        DEBUG_CAPTURE[k] = v


def get_program(key, builder):
    if key not in _PROG_CACHE:
        _PROG_CACHE[key] = builder()
    return _PROG_CACHE[key]


def _launch(envnc, in_maps):
    env, nc = envnc
    res = run_bass_kernel_spmd(nc, in_maps, core_ids=list(range(NCORES)))
    return res.results


def kernel(L0_attn_X, L0_attn_ln_g, L0_attn_ln_b, L0_attn_Wq, L0_attn_bq,
           L0_attn_Wk, L0_attn_bk, L0_attn_Wv, L0_attn_bv, L0_attn_Wo,
           L0_attn_bo, L0_ffn_w1, L0_ffn_w2, L0_ffn_w3, L1_attn_W_qkv,
           L1_attn_W_proj, L1_ffn_W_up, L1_ffn_W_gate, L1_ffn_W_down):
    X = _f32(L0_attn_X).reshape(B * S, D)
    rows = [shard_row_indices(c) for c in range(NCORES)]
    xsh = [np.ascontiguousarray(X[rows[c]]).reshape(-1) for c in range(NCORES)]

    mats = {
        "wq": _f32(L0_attn_Wq), "wk": _f32(L0_attn_Wk), "wv": _f32(L0_attn_Wv),
        "wo": _f32(L0_attn_Wo),
        "qkv": _f32(np.asarray(L1_attn_W_qkv).T),
        "proj": _f32(np.asarray(L1_attn_W_proj).T),
        "w1": _f32(np.asarray(L0_ffn_w1).T), "w3": _f32(np.asarray(L0_ffn_w3).T),
        "w2": _f32(np.asarray(L0_ffn_w2).T),
        "up": _f32(np.asarray(L1_ffn_W_up).T), "gate": _f32(np.asarray(L1_ffn_W_gate).T),
        "down": _f32(np.asarray(L1_ffn_W_down).T),
    }
    msl = {}
    for m, (K, N, law, layout) in PREP.items():
        rs = K // NCORES
        msl[m] = [np.ascontiguousarray(mats[m][c * rs:(c + 1) * rs]).reshape(-1)
                  for c in range(NCORES)]

    ident = np.eye(128, dtype=BF)
    ident32 = np.eye(128, dtype=np.float32)
    ones1 = np.ones((1, 128), dtype=BF)
    masks = [host_masks(c) for c in range(NCORES)]
    lng = np.ascontiguousarray(np.broadcast_to(_f32(L0_attn_ln_g), (128, D)))
    lnb = np.ascontiguousarray(np.broadcast_to(_f32(L0_attn_ln_b), (128, D)))
    bq_rs = np.ascontiguousarray(_f32(L0_attn_bq).reshape(16, 128).T)
    bk_rs = np.ascontiguousarray(_f32(L0_attn_bk).reshape(16, 128).T)
    bvrow = np.asarray(L0_attn_bv, dtype=BF).reshape(1, D)
    borow = np.asarray(L0_attn_bo, dtype=BF).reshape(1, D)

    if FUSED:
        xcur = kernel_fused((xsh, masks, ident, ones1, lng, lnb, bq_rs, bk_rs,
                             bvrow, borow, msl))
        out = np.empty((B * S, D), np.float32)
        for c in range(NCORES):
            out[rows[c]] = xcur[c].reshape(TOK, D)
        return out.reshape(B, S, D)

    def build_w1():
        drams = [("part_out", [8, 1], F32, "ExternalOutput"),
                 ("wsum_scr", [128, 8], F32, "Internal")]
        for m in QMATS:
            K, N, law, layout = PREP[m]
            drams.append(("qsl_" + m, [K // NCORES * N], F32, "ExternalInput"))
        return build_program(["VEC", "WIN"], drams, body_w1)

    envnc = get_program("w1", build_w1)
    ins = [dict(**{"qsl_" + m: msl[m][c] for m in QMATS})
           for c in range(NCORES)]
    res = _launch(envnc, ins)
    partials = np.zeros((1, 48), np.float32)
    for c in range(NCORES):
        pc = np.asarray(res[c]["part_out"], np.float32).reshape(8)
        for mi in range(6):
            partials[0, mi * 8 + c] = pc[mi]
    _cap("partials", partials)
    partials = np.ascontiguousarray(np.broadcast_to(partials, (128, 48)))

    def build_w2():
        drams = [("partials", [128, 48], F32, "ExternalInput")]
        for m, (K, N, law, layout) in PREP.items():
            drams.append(("qsl_" + m, [K // NCORES * N], F32, "ExternalInput"))
            drams.append(("qout_" + m, [K // NCORES * N], BF16, "ExternalOutput"))
        return build_program(["SCL", "RSB", "WIN", "WOUT"], drams, body_w2)

    envnc = get_program("w2", build_w2)
    ins = [dict(partials=partials, **{"qsl_" + m: msl[m][c] for m in PREP})
           for c in range(NCORES)]
    res = _launch(envnc, ins)
    wprep = {}
    for m in PREP:
        wprep[m] = np.concatenate(
            [np.asarray(res[c]["qout_" + m]).reshape(-1) for c in range(NCORES)])
    _cap("wprep", wprep)

    def run_layer(layer, xcur):
        def build_a1(layer=layer):
            drams = [("xin_d", [TOK * D], F32, "ExternalInput"),
                     ("IDENT_d", [128, 128], BF16, "ExternalInput"),
                     ("ksh", [2048 * TOK], BF16, "ExternalOutput"),
                     ("vsh", [TOK * D], BF16, "ExternalOutput"),
                     ("qsb_out", [128, 16384], BF16, "ExternalOutput")]
            bufs = ["XTILE", "BIGA", "BIGB", "SCR", "SCR2", "XNT0", "XNT1",
                    "VEC", "IDENT", "WSTR", "WHALF", "STG0", "STG1"]
            consts = ["IDENT"]
            if layer == 0:
                drams += [("wq_b", [D * D], BF16, "ExternalInput"),
                          ("wk_b", [D * D], BF16, "ExternalInput"),
                          ("wv_b", [D * D], BF16, "ExternalInput"),
                          ("LNG_d", [128, D], F32, "ExternalInput"),
                          ("LNB_d", [128, D], F32, "ExternalInput"),
                          ("BQ_d", [128, 16], F32, "ExternalInput"),
                          ("BK_d", [128, 16], F32, "ExternalInput"),
                          ("BVROW_d", [1, D], BF16, "ExternalInput")]
                bufs += ["LNG", "LNB", "BQ", "BK", "BVROW", "ONES1"]
                drams.append(("ONES1_d", [1, 128], BF16, "ExternalInput"))
                consts += ["LNG", "LNB", "BQ", "BK", "BVROW", "ONES1"]
            else:
                drams.append(("qkv_b", [D * 3 * D], BF16, "ExternalInput"))
            return build_program(bufs, drams, lambda env: body_lnqkv(env, layer),
                                 load_consts=tuple(consts))

        envnc = get_program(f"a1_{layer}", build_a1)
        if layer == 0:
            ins = [dict(xin_d=xcur[c], IDENT_d=ident, wq_b=wprep["wq"],
                        wk_b=wprep["wk"], wv_b=wprep["wv"], LNG_d=lng, LNB_d=lnb,
                        BQ_d=bq_rs, BK_d=bk_rs, BVROW_d=bvrow, ONES1_d=ones1)
                   for c in range(NCORES)]
        else:
            ins = [dict(xin_d=xcur[c], IDENT_d=ident, qkv_b=wprep["qkv"])
                   for c in range(NCORES)]
        res = _launch(envnc, ins)
        kall = np.concatenate([np.asarray(res[c]["ksh"]).reshape(-1) for c in range(NCORES)])
        vall = np.concatenate([np.asarray(res[c]["vsh"]).reshape(-1) for c in range(NCORES)])
        qsb = [np.asarray(res[c]["qsb_out"]) for c in range(NCORES)]
        _cap(f"kall{layer}", kall); _cap(f"vall{layer}", vall); _cap(f"qsb{layer}", qsb)

        def build_a2():
            drams = [("qsb_in", [128, 16384], BF16, "ExternalInput"),
                     ("kall", [NCORES * 2048 * TOK], BF16, "ExternalInput"),
                     ("vall", [NCORES * TOK * D], BF16, "ExternalInput"),
                     ("IDENT_d", [128, 128], BF16, "ExternalInput"),
                     ("MASK_d", [128, 1024], BF16, "ExternalInput"),
                     ("avf_out", [128, 16384], BF16, "ExternalOutput")]
            bufs = ["BIGA", "BIGB", "KSB", "VSB0", "VSB1", "AT0", "AT1",
                    "AVT0", "AVT1", "VEC", "MASK", "IDENT", "MROW", "R2",
                    "ONESC", "ONES1"]
            drams.append(("ONES1_d", [1, 128], BF16, "ExternalInput"))
            return build_program(bufs, drams, body_attn,
                                 load_consts=("IDENT", "MASK", "ONES1"))

        envnc = get_program("a2", build_a2)
        ins = [dict(qsb_in=qsb[c], kall=kall, vall=vall, IDENT_d=ident,
                    MASK_d=masks[c], ONES1_d=ones1) for c in range(NCORES)]
        res = _launch(envnc, ins)
        avf = [np.asarray(res[c]["avf_out"]) for c in range(NCORES)]
        _cap(f"avf{layer}", avf)

        def build_op(layer=layer):
            drams = [("avf_in", [128, 16384], BF16, "ExternalInput"),
                     ("xin_d", [TOK * D], F32, "ExternalInput"),
                     ("xout_d", [TOK * D], F32, "ExternalOutput")]
            bufs = ["XTILE", "BIGA", "WHALF", "VEC"]
            consts = []
            if layer == 0:
                drams += [("wo_b", [D * D], BF16, "ExternalInput"),
                          ("ONES1_d", [1, 128], BF16, "ExternalInput"),
                          ("BOROW_d", [1, D], BF16, "ExternalInput")]
                bufs += ["ONES1", "BOROW"]
                consts = ["ONES1", "BOROW"]
            else:
                drams.append(("proj_b", [D * D], BF16, "ExternalInput"))
            return build_program(bufs, drams, lambda env: body_oproj(env, layer),
                                 load_consts=tuple(consts))

        envnc = get_program(f"op_{layer}", build_op)
        if layer == 0:
            ins = [dict(avf_in=avf[c], xin_d=xcur[c], wo_b=wprep["wo"],
                        ONES1_d=ones1, BOROW_d=borow) for c in range(NCORES)]
        else:
            ins = [dict(avf_in=avf[c], xin_d=xcur[c], proj_b=wprep["proj"])
                   for c in range(NCORES)]
        res = _launch(envnc, ins)
        x1 = [np.asarray(res[c]["xout_d"], np.float32) for c in range(NCORES)]
        _cap(f"xattn{layer}", x1)

        if layer == 0:
            def build_f0():
                drams = [("xin_d", [TOK * D], F32, "ExternalInput"),
                         ("xout_d", [TOK * D], F32, "ExternalOutput"),
                         ("partials", [128, 48], F32, "ExternalInput"),
                         ("w1q_b", [D * H0], BF16, "ExternalInput"),
                         ("w3q_b", [D * H0], BF16, "ExternalInput"),
                         ("w2q_rm", [H0 * D], BF16, "ExternalInput"),
                         ("IDENT_d", [128, 128], BF16, "ExternalInput")]
                bufs = ["XTILE", "BIGA", "SCR", "SCR2", "XNT0", "XNT1", "VEC",
                        "SCL", "RSB", "IDENT", "WFB", "SILU", "HB", "HQ",
                        "HQT", "WB20"]

                def body(env):
                    env.hgroups = [(j, 1) for j in range(11)]
                    env.nwb2 = 1
                    body_ffn0(env)
                return build_program(bufs, drams, body, load_consts=("IDENT",))

            envnc = get_program("f0", build_f0)
            ins = [dict(xin_d=x1[c], partials=partials, w1q_b=wprep["w1"],
                        w3q_b=wprep["w3"], w2q_rm=wprep["w2"], IDENT_d=ident)
                   for c in range(NCORES)]
        else:
            def build_f1():
                drams = [("xin_d", [TOK * D], F32, "ExternalInput"),
                         ("xout_d", [TOK * D], F32, "ExternalOutput"),
                         ("partials", [128, 48], F32, "ExternalInput"),
                         ("upq_b", [D * H1], BF16, "ExternalInput"),
                         ("gateq_b", [D * H1], BF16, "ExternalInput"),
                         ("downq_rm", [H1 * D], BF16, "ExternalInput"),
                         ("IDENT_d", [128, 128], BF16, "ExternalInput")]
                bufs = ["XTILE", "BIGA", "BIGB", "SCR", "XNT0", "XNT1", "VEC",
                        "SCL", "RSB", "IDENT", "WUPJ", "WGJ", "WDNH", "SILU1"]

                def body(env):
                    body_ffn1(env)
                return build_program(bufs, drams, body, load_consts=("IDENT",))

            envnc = get_program("f1", build_f1)
            ins = [dict(xin_d=x1[c], partials=partials, upq_b=wprep["up"],
                        gateq_b=wprep["gate"], downq_rm=wprep["down"],
                        IDENT_d=ident) for c in range(NCORES)]
        res = _launch(envnc, ins)
        xf = [np.asarray(res[c]["xout_d"], np.float32) for c in range(NCORES)]
        _cap(f"xffn{layer}", xf)
        return xf

    xcur = xsh
    xcur = run_layer(0, xcur)
    xcur = run_layer(1, xcur)

    out = np.empty((B * S, D), np.float32)
    for c in range(NCORES):
        out[rows[c]] = xcur[c].reshape(TOK, D)
    return out.reshape(B, S, D)


ARENA_B = 65536

OVERLAYS = {
    "WIN":   ([128, 8192], F32, 0),
    "WOUT":  ([128, 8192], BF16, 32768),
    "SCR":   ([128, 2048], F32, 0),
    "SCR2":  ([128, 2048], F32, 8192),
    "XNT0":  ([128, 2048], BF16, 16384),
    "XNT1":  ([128, 2048], BF16, 20480),
    "LNG":   ([128, 2048], F32, 24576),
    "LNB":   ([128, 2048], F32, 32768),
    "WSTR":  ([128, 8192], BF16, 0),
    "STG0":  ([128, 2048], BF16, 16384),
    "STG1":  ([128, 2048], BF16, 20480),
    "WHALF": ([128, 16384], BF16, 24576),
    "KSB":   ([128, 8192], BF16, 0),
    "VSB0":  ([128, 4160], BF16, 16384),
    "VSB1":  ([128, 4160], BF16, 24704),
    "AT0":   ([128, 1408], BF16, 33024),
    "AT1":   ([128, 1408], BF16, 35840),
    "AVT0":  ([128, 128], BF16, 38656),
    "AVT1":  ([128, 128], BF16, 38912),
    "MROW":  ([1, 1024], F32, 39424),
    "R2":    ([1, 1536], BF16, 43520),
    "WFB":   ([128, 8192], BF16, 0),
    "WB20":  ([128, 8192], BF16, 0),
    "SILU":  ([128, 11264], BF16, 16384),
    "HQ":    ([128, 5632], BF16, 16384),
    "HB":    ([128, 11264], BF16, 38912),
    "HQT":   ([128, 512], BF16, 61440),
    "WUPJ":  ([128, 8192], BF16, 0),
    "WGJ":   ([128, 8192], BF16, 16384),
    "WDNH":  ([128, 8192], BF16, 32768),
    "SILU1": ([128, 1024], F32, 49152),
}


def emit_wsums_fused(env):
    p, h = env.prog, env.h
    acc, win, scl = h["VEC"], h["WIN"], h["SCL"]
    p.op("dve", lambda e: e.memset(acc[:, 16:25], 0.0), writes=["wacc"])
    for mi, m in enumerate(QMATS):
        K, N, law, layout = PREP[m]
        rows = K // NCORES
        sl = h["qsl_" + m]
        for t in range((rows + 127) // 128):
            r0 = t * 128
            pr = min(128, rows - r0)
            src = AP_(sl, r0 * N, [[N, pr], [1, N]])
            p.op("sp", lambda e, src=src, pr=pr, N=N: e.dma_start(
                out=win[0:pr, 0:N], in_=src), writes=["WIN"], inc=16)
            p.op("dve", lambda e, pr=pr, N=N: e.tensor_reduce(
                acc[0:pr, 24:25], win[0:pr, 0:N], AX.X, ALU.add,
                apply_absolute_value=True),
                reads=["WIN"], writes=["wacc"])
            p.op("dve", lambda e, pr=pr, mi=mi: e.tensor_add(
                acc[0:pr, 16 + mi:17 + mi], acc[0:pr, 16 + mi:17 + mi],
                acc[0:pr, 24:25]),
                reads=["wacc"], writes=["wacc"])
    p.op("sp", lambda e: e.dma_start(out=h["wsacc_d"][:, :], in_=acc[:, 16:24]),
         reads=["wacc"], writes=["wsacc_d"], inc=16)
    p.op("pool", lambda e: e.collective_compute(
        "AllReduce", ALU.add, replica_groups=[list(range(NCORES))],
        ins=[h["wsacc_d"][:, :]], outs=[h["wsg"][:, :]]),
        reads=["wsacc_d"], writes=["wsg"], xsem=True)
    p.op("sp", lambda e: e.dma_start(
        out=win[0:3, 0:128], in_=AP_(h["wsg"], 0, [[1, 3], [8, 128]]),
        allow_slow_non_contiguous=True),
        reads=["wsg"], writes=["WIN"], inc=16)
    p.op("sp", lambda e: e.dma_start(
        out=win[32:35, 0:128], in_=AP_(h["wsg"], 3, [[1, 3], [8, 128]]),
        allow_slow_non_contiguous=True),
        reads=["wsg"], writes=["WIN"], inc=16)
    p.op("dve", lambda e: e.reduce_sum(scl[0:3, 0:1], win[0:3, 0:128], AX.X),
         reads=["WIN"], writes=["SCL"])
    p.op("dve", lambda e: e.reduce_sum(scl[32:35, 0:1], win[32:35, 0:128], AX.X),
         reads=["WIN"], writes=["SCL"])
    p.op("dve", lambda e: e.tensor_mul(scl[:, 1:2], scl[:, 0:1], h["SIZESV"][:, 0:1]),
         reads=["SCL", "SIZESV"], writes=["SCL"])
    p.op("dve", lambda e: e.tensor_scalar_max(scl[0:3, 2:3], scl[0:3, 1:2], EPS5),
         reads=["SCL"], writes=["SCL"])
    p.op("dve", lambda e: e.tensor_scalar_add(scl[32:35, 2:3], scl[32:35, 1:2], EPS8),
         reads=["SCL"], writes=["SCL"])
    p.op("dve", lambda e: e.reciprocal(scl[0:3, 3:4], scl[0:3, 2:3]),
         reads=["SCL"], writes=["SCL"])
    p.op("dve", lambda e: e.reciprocal(scl[32:35, 3:4], scl[32:35, 2:3]),
         reads=["SCL"], writes=["SCL"])
    for (p0, c0, dcol) in ((0, 2, 0), (32, 2, 3), (0, 3, 8), (32, 3, 11)):
        p.op("sp", lambda e, p0=p0, c0=c0, dcol=dcol: e.dma_start(
            out=AP_(h["scl_dram"], dcol, [[1, 3], [1, 1]]),
            in_=scl[p0:p0 + 3, c0:c0 + 1]),
            reads=["SCL"], writes=["scl_dram"], inc=16)
    p.op("sp", lambda e: e.dma_start(out=h["SCLROW"][0:1, 0:16], in_=h["scl_dram"][:, :]),
         reads=["scl_dram"], writes=["SCLROW"], inc=16)
    p.op("pe", lambda e: e.matmul(h["PA0"][:, 0:16], h["ONESF"][:, :],
                                  h["SCLROW"][0:1, 0:16], start=True, stop=True),
         reads=["SCLROW", "ONESF"], writes=["PA0"])
    p.op("dve", lambda e: e.tensor_copy(h["RSB"][:, 0:16], h["PA0"][:, 0:16]),
         reads=["PA0"], writes=["RSB"])


def collective_ag(env, in_name, out_name, in_shape2d):
    p, h = env.prog, env.h
    ia = AP_(h[in_name], 0, [[in_shape2d[1], in_shape2d[0]], [1, in_shape2d[1]]])
    oa = AP_(h[out_name], 0, [[in_shape2d[1], in_shape2d[0] * NCORES], [1, in_shape2d[1]]])
    p.op("pool", lambda e, ia=ia, oa=oa: e.collective_compute(
        "AllGather", ALU.bypass, replica_groups=[list(range(NCORES))],
        ins=[ia], outs=[oa]),
        reads=[in_name], writes=[out_name], xsem=True)


def build_fused(debug=False):
    nc = bass.Bass("TRN2", target_bir_lowering=False)
    env = Env(nc)
    setup_xr(env, "resident")
    h = env.h
    env.dram("xin_d", [TOK * D], F32, kind="ExternalInput")
    env.dram("out_d", [TOK * D], F32, kind="ExternalOutput")
    for cn, shape, dt in (("IDENT_d", [128, 128], BF16), ("MASK_d", [128, 1024], BF16),
                          ("ONES1_d", [1, 128], BF16), ("ONESF_d", [1, 128], F32),
                          ("BQ_d", [128, 16], F32), ("BK_d", [128, 16], F32),
                          ("LNG_d", [128, D], F32), ("LNB_d", [128, D], F32),
                          ("BVROW_d", [1, D], BF16), ("BOROW_d", [1, D], BF16),
                          ("SIZESV_d", [128, 1], F32)):
        env.dram(cn, shape, dt, kind="ExternalInput")
    for m, (K, N, law, layout) in PREP.items():
        env.dram("qsl_" + m, [K // NCORES * N], F32, kind="ExternalInput")
        env.dram("qsh_" + m, [K // NCORES * N], BF16, kind="Internal")
        env.dram("qg_" + m, [K * N], BF16, kind="Internal")
    env.dram("wsacc_d", [128, 8], F32, kind="Internal")
    env.dram("wsg", [128, 8], F32, kind="Internal")
    env.dram("scl_dram", [1, 16], F32, kind="Internal")
    if debug:
        env.dram("dbg_rsb", [128, 16], F32, kind="ExternalOutput")
        for t in ("dbg_xa0", "dbg_xf0", "dbg_xa1"):
            env.dram(t, [TOK * D], F32, kind="ExternalOutput")
        env.dram("dbg_avf0", [128, 16384], BF16, kind="ExternalOutput")
        env.dram("dbg_avf1", [128, 16384], BF16, kind="ExternalOutput")
        env.dram("dbg_q0", [128, 16384], BF16, kind="ExternalOutput")
        env.dram("dbg_kall", [NCORES * 2048 * TOK], BF16, kind="ExternalOutput")
        env.dram("dbg_vall", [NCORES * TOK * D], BF16, kind="ExternalOutput")
        env.dram("dbg_mrow", [1, 1024], F32, kind="ExternalOutput")
        env.dram("dbg_r2", [1, 1536], BF16, kind="ExternalOutput")
        env.dram("dbg_at0", [128, 1408], BF16, kind="ExternalOutput")
    for L in (0, 1):
        env.dram(f"ksh{L}", [2048 * TOK], BF16, kind="Internal")
        env.dram(f"vsh{L}", [TOK * D], BF16, kind="Internal")
        env.dram(f"kall{L}", [NCORES * 2048 * TOK], BF16, kind="Internal")
        env.dram(f"vall{L}", [NCORES * TOK * D], BF16, kind="Internal")

    with ExitStack() as ctx:
        for nm in ("XR", "BIGA", "BIGB", "VEC", "SCL", "RSB", "IDENT", "ONES1",
                   "ONESC", "MASK", "BQ", "BK", "BVROW", "BOROW"):
            env.sb(ctx, nm, *BUFS[nm])
        env.sb(ctx, "ONESF", [1, 128], F32)
        env.sb(ctx, "SCLROW", [1, 16], F32)
        arena = ctx.enter_context(nc.sbuf_tensor("ARENA", [128, ARENA_B], mybir.dt.uint8))
        arena_base = nc.lookup_mloc(arena).addr
        for nm, (shape, dt, off) in OVERLAYS.items():
            h[nm] = nc.alloc_sbuf_tensor_at(nm, shape, dt, offset=arena_base + off)
        for i in range(4):
            env.ps(ctx, f"PA{i}", [128, 512], F32)
        env.ps(ctx, "PB", [128, 1536], F32)
        env.ps(ctx, "PT", [128, 1024], BF16)
        p = env.prog
        CON = env.sb(ctx, "CONSTS", [128, 4], F32)
        p.op("pool", lambda e: e.memset(CON[:, 0:1], MAGIC), writes=["CONSTS"])
        nc.const_aps.register(F32, MAGIC, CON[:, 0:1])

        for cn in ("IDENT", "MASK", "ONES1", "ONESF", "BQ", "BK", "BVROW", "BOROW"):
            hsb, hdr = h[cn], h[cn + "_d"]
            p.op("sp", lambda e, hsb=hsb, hdr=hdr: e.dma_start(
                out=hsb[:, :], in_=hdr[:, :]), writes=[cn], inc=16)
        env.sb(ctx, "SIZESV", [128, 1], F32)
        p.op("sp", lambda e: e.dma_start(
            out=h["SIZESV"][:, :], in_=h["SIZESV_d"][:, :]),
            writes=["SIZESV"], inc=16)

        xsrc = AP_(h["xin_d"], 0, [[D, 128], [128 * D, NT], [1, D]])
        xdst = AP_(h["XR"], 0, [[16384, 128], [D, NT], [1, D]])
        p.op("sp", lambda e: e.dma_start(out=xdst, in_=xsrc), writes=["XR"], inc=16)

        emit_wsums_fused(env)
        p.fence()
        for m in PREP:
            h["qout_" + m] = h["qsh_" + m]
        if debug:
            p.op("sp", lambda e: e.dma_start(out=h["dbg_rsb"][:, :], in_=h["RSB"][:, :]),
                 reads=["RSB"], writes=["dbg_rsb"], inc=16)
        emit_prep(env)
        p.fence()
        for m, (K, N, law, layout) in PREP.items():
            collective_ag(env, "qsh_" + m, "qg_" + m, [K // NCORES * N // 512, 512])
        p.fence()

        for layer in (0, 1):
            if layer == 0:
                h["wq_b"], h["wk_b"], h["wv_b"], h["wo_b"] = \
                    h["qg_wq"], h["qg_wk"], h["qg_wv"], h["qg_wo"]
                for cn in ("LNG", "LNB"):
                    hsb, hdr = h[cn], h[cn + "_d"]
                    p.op("sp", lambda e, hsb=hsb, hdr=hdr: e.dma_start(
                        out=hsb[:, :], in_=hdr[:, :]), writes=[cn], inc=16)
            else:
                h["qkv_b"], h["proj_b"] = h["qg_qkv"], h["qg_proj"]
            h["ksh"], h["vsh"] = h[f"ksh{layer}"], h[f"vsh{layer}"]
            h["kall"], h["vall"] = h[f"kall{layer}"], h[f"vall{layer}"]
            emit_lnqkv(env, layer)
            collective_ag(env, f"ksh{layer}", f"kall{layer}", [2048, TOK])
            collective_ag(env, f"vsh{layer}", f"vall{layer}", [TOK, D])
            h["kall"], h["vall"] = h[f"kall{layer}"], h[f"vall{layer}"]
            if debug:
                p.op("sp", lambda e, layer=layer: e.dma_start(
                    out=h["dbg_q0"][:, :] if layer == 0 else h["dbg_avf1"][:, :],
                    in_=h["BIGB"][:, :]), reads=["BIGB"], writes=["dbgq"], inc=16)
            p.fence()
            if debug and layer == 0:
                p.op("sp", lambda e: e.dma_start(
                    out=h["dbg_kall"].ap(), in_=h["kall0"].ap()),
                    reads=["kall0"], writes=["dbgk"], inc=16)
                p.op("sp", lambda e: e.dma_start(
                    out=h["dbg_vall"].ap(), in_=h["vall0"].ap()),
                    reads=["vall0"], writes=["dbgv"], inc=16)
            emit_attn(env)
            if debug and layer == 0:
                p.op("sp", lambda e: e.dma_start(out=h["dbg_avf0"][:, :], in_=h["BIGA"][:, :]),
                     reads=["BIGA"], writes=["dbgavf"], inc=16)
                p.op("sp", lambda e: e.dma_start(out=h["dbg_mrow"][:, :], in_=h["MROW"][:, :]),
                     reads=["MROW"], writes=["dbgm"], inc=16)
                p.op("sp", lambda e: e.dma_start(out=h["dbg_r2"][:, :], in_=h["R2"][:, :]),
                     reads=["R2"], writes=["dbgr"], inc=16)
                p.op("sp", lambda e: e.dma_start(out=h["dbg_at0"][:, :], in_=h["AT0"][:, :]),
                     reads=["AT0"], writes=["dbga"], inc=16)
            emit_oproj(env, layer)
            if debug:
                dn = "dbg_xa0" if layer == 0 else "dbg_xa1"
                dst = AP_(h[dn], 0, [[D, 128], [128 * D, NT], [1, D]])
                src = AP_(h["XR"], 0, [[16384, 128], [D, NT], [1, D]])
                p.op("sp", lambda e, dst=dst, src=src: e.dma_start(out=dst, in_=src),
                     reads=["XR"], writes=[dn], inc=16)
            if layer == 0:
                h["w1q_b"], h["w3q_b"], h["w2q_rm"] = h["qg_w1"], h["qg_w3"], h["qg_w2"]
                env.hgroups = [(j, 1) for j in range(11)]
                env.nwb2 = 1
                emit_ffn0(env)
                if debug:
                    dst = AP_(h["dbg_xf0"], 0, [[D, 128], [128 * D, NT], [1, D]])
                    src = AP_(h["XR"], 0, [[16384, 128], [D, NT], [1, D]])
                    p.op("sp", lambda e, dst=dst, src=src: e.dma_start(out=dst, in_=src),
                         reads=["XR"], writes=["dbg_xf0"], inc=16)
            else:
                h["upq_b"], h["gateq_b"], h["downq_rm"] = \
                    h["qg_up"], h["qg_gate"], h["qg_down"]
                emit_ffn1(env)
        p.fence()
        osrc = AP_(h["XR"], 0, [[16384, 128], [D, NT], [1, D]])
        odst = AP_(h["out_d"], 0, [[D, 128], [128 * D, NT], [1, D]])
        p.op("sp", lambda e: e.dma_start(out=odst, in_=osrc), reads=["XR"],
             writes=["out_d"], inc=16)
        p.fin("sp", "sp")
        p.fin("sp", "pool")
        p.emit(ctx)
    return env, nc


FUSED = True


def kernel_fused(inputs_host):
    (xsh, masks, ident, ones1, lng, lnb, bq_rs, bk_rs, bvrow, borow, msl) = inputs_host
    envnc = get_program("fused", build_fused)
    onesf = np.ones((1, 128), np.float32)
    sizesv = np.zeros((128, 1), np.float32)
    for i, m in enumerate(QMATS):
        K, N = PREP[m][0], PREP[m][1]
        pi = i if i < 3 else 32 + (i - 3)
        sizesv[pi, 0] = 1.0 / (K * N)
    ins = []
    for c in range(NCORES):
        d = dict(xin_d=xsh[c], IDENT_d=ident, MASK_d=masks[c], ONES1_d=ones1,
                 ONESF_d=onesf, BQ_d=bq_rs, BK_d=bk_rs, LNG_d=lng, LNB_d=lnb,
                 BVROW_d=bvrow, BOROW_d=borow, SIZESV_d=sizesv)
        for m in PREP:
            d["qsl_" + m] = msl[m][c]
        ins.append(d)
    res = _launch(envnc, ins)
    return [np.asarray(res[c]["out_d"], np.float32) for c in range(NCORES)]
